# version 13 (final)
# speedup vs baseline: 1.3491x; 1.0445x over previous
import numpy as np
import ml_dtypes
from contextlib import ExitStack

import concourse.bass as bass
import concourse.mybir as mybir
from concourse.bass_utils import run_bass_kernel_spmd

F32 = mybir.dt.float32
BF16 = mybir.dt.bfloat16
F8 = mybir.dt.float8e4
AF = mybir.ActivationFunctionType
ALU = mybir.AluOpType
AX = mybir.AxisListType.X

B, T, C, H, HS, HID = 4, 4096, 1024, 16, 64, 4096
NCH = C // 128
BLK = 512
OWN_BLOCKS = ([0, 2, 4, 6], [1, 3, 5, 7])
OWN_T = T // 2
MAGIC = 12582912.0
EPS = 1e-5
DBGM = False
_QN = True

bfdt = ml_dtypes.bfloat16
ENGINES = ("sync", "vector", "scalar", "tensor", "gpsimd")


class Prog:

    def __init__(self, nc, es):
        self.nc, self.es = nc, es
        self.q = {e: [] for e in ENGINES}
        self.sems = {}

    def _sem(self, name):
        if name not in self.sems:
            self.sems[name] = [self.es.enter_context(self.nc.semaphore(name)), 0]
        return self.sems[name]

    def emit(self, eng, fn, wait=(), inc=None, inc_k=1):
        tok = None
        if inc is not None:
            s = self._sem(inc)
            s[1] += inc_k
            tok = (inc, s[1])
        self.q[eng].append((fn, tuple(w for w in wait if w is not None), inc, inc_k))
        return tok

    def tok(self, name):
        return (name, self._sem(name)[1])

    def build(self):
        block = self.es.enter_context(self.nc.Block())
        last_wait = {e: {} for e in ENGINES}

        def mk(engname):
            def section(eng):
                lw = last_wait[engname]
                for fn, waits, inc, inc_k in self.q[engname]:
                    for semname, val in waits:
                        if lw.get(semname, -1) >= val:
                            continue
                        eng.wait_ge(self.sems[semname][0], val)
                        lw[semname] = val
                    if fn is None:
                        continue
                    instr = fn(eng)
                    if inc is not None:
                        instr.then_inc(self.sems[inc][0], inc_k)
            return section

        for engname in ENGINES:
            if self.q[engname]:
                getattr(block, engname)(mk(engname))


class Ring:
    def __init__(self, bufs):
        self.bufs = bufs
        self.war = [None] * len(bufs)
        self.i = -1

    def next(self):
        self.i = (self.i + 1) % len(self.bufs)
        return self.bufs[self.i], self.war[self.i]

    def next_i(self):
        self.i = (self.i + 1) % len(self.bufs)
        return self.bufs[self.i], self.war[self.i], self.i

    def used(self, tok, idx=None):
        self.war[self.i if idx is None else idx] = tok


def ln_emit(p, x_tile, n, stat, scr, wait=()):
    inv_n = 1.0 / n
    m, r, s2 = stat[:, 0:1], stat[:, 1:2], stat[:, 2:3]
    t0 = p.emit("vector", lambda e: e.tensor_reduce(
        out=m, in_=x_tile, axis=AX, op=ALU.add), wait=wait, inc="v")
    t1 = p.emit("vector", lambda e: e.scalar_tensor_tensor(
        out=scr, in0=x_tile, scalar=1.0, in1=x_tile, op0=ALU.bypass, op1=ALU.mult,
        accum_out=s2), inc="v")
    t2 = p.emit("vector", lambda e: e.tensor_scalar(
        out=m, in0=m, scalar1=inv_n, scalar2=None, op0=ALU.mult), wait=[t0], inc="v")
    t3 = p.emit("vector", lambda e: e.tensor_tensor(
        out=r, in0=m, in1=m, op=ALU.mult), wait=[t2], inc="v")
    tv = p.emit("vector", lambda e: e.scalar_tensor_tensor(
        out=r, in0=s2, scalar=inv_n, in1=r, op0=ALU.mult, op1=ALU.subtract),
        wait=[t1, t3], inc="v")
    ta = p.emit("scalar", lambda e: e.activation(
        out=r, in_=r, func=AF.Sqrt, bias=np.float32(EPS)), wait=[tv], inc="a")
    return p.emit("vector", lambda e: e.reciprocal(out=r, in_=r), wait=[ta], inc="v")


def bcast_ap(dram_handle, off, nrows, ncols):
    return bass.AP(dram_handle, off, [[0, nrows], [1, ncols]])



def emit_l0_mixer(nc, p, sbes, io, tag="m1"):
    sb = lambda name, shape, dt=F32: sbes.enter_context(
        nc.sbuf_tensor(tag + name, shape, dt))
    pst = lambda name, shape, dt=F32: sbes.enter_context(
        nc.psum_tensor(tag + name, shape, dt))

    x_d = io["x"]
    xq_d = io["xq"]
    wqk_d = io["wqk"]
    wv_d = io["wv"]
    wp_d = io["wp"]
    bp_dh = io["bp_h"]
    mask_d = io["masks"]
    id_d = io["ident"]
    s_out_d = io["s_out"]
    q_scr = io["qscr_h"]

    VSTR = 1040
    k_fm = sb("k_fm", [128, NCH * T], BF16)
    v_tm = sb("v_tm", [128, 32 * VSTR], BF16)
    arena = sb("arena", [128, 16384], BF16)
    wqkb = arena[:, 0:16384]
    wvb = arena[:, 0:8192]
    wpb = sb("wpb", [128, NCH * C], BF16)
    wldf = sb("wldf", [128, 1024], F32)
    xnf = sb("xnf", [128, NCH * 512], BF16)
    xr = [sb(f"xr{i}", [128, C], F32) for i in range(2)]
    xnb = [sb("xnb0", [128, C], BF16)]
    stat = sb("stat", [128, 8], F32)
    scr = sb("scr", [128, C], F32)
    ident = sb("identsb", [128, 128], BF16)
    ones1 = sb("ones1", [1, 128], F32)
    onescol = sb("onescol", [128, 2], BF16)
    onescf = sb("onescf", [128, 1], F32)
    lnkm = sb("lnkm", [1, 16], F32)
    kmrow = sb("kmrow", [1, 16], F32)
    mtmp = sb("mtmp", [1, 8], F32)
    qm2 = sb("qm2", [1, 64], F32)
    mrow = sb("mrow", [1, 64], F32)
    lnkm4 = sb("lnkm4", [1, 64], F32)
    mcol64 = sb("mcol64", [128, 64], F32)
    qring_t = [bass.AP(arena, 0, [[16384, 128], [1, 4096]]),
               bass.AP(arena, 4096, [[16384, 128], [1, 4096]])]
    probs_t = ([bass.AP(arena, 8192 + 512 * i, [[16384, 128], [1, 512]])
                for i in range(4)] +
               [bass.AP(arena, 14336 + 512 * i, [[16384, 128], [1, 512]])
                for i in range(4)])
    a_un = sb("a_un", [128, 512], F32)
    a_bf = bass.AP(arena, 10240, [[16384, 128], [1, 4096]])
    qtmp = wpb[:, 0:NCH * 512]
    dst2 = None
    maskb = bass.AP(xnf, 0, [[NCH * 512, 128], [1, 4096]])
    bpb = bass.AP(scr, 0, [[C, 128], [1, C]])
    so = xr

    sc_ps = [pst(f"scps{i}", [128, 512], F32) for i in range(4)]
    av_ps = [pst(f"avps{i}", [128, 512], F32) for i in range(2)]
    bc_ps = pst("bcps", [128, 512], F32)
    g_ps = [sc_ps[0], sc_ps[1]]
    tr_ps = av_ps[1][:].bitcast(BF16)[:, 0:512]
    dst2 = wldf[0:1, 0:1024]

    LD, V, A, TT, G = "ld", "v", "a", "t", "g"

    def din(dst, src, wait=(), sem="ld"):
        return p.emit("sync", lambda e: e.dma_start(out=dst, in_=src), wait=wait,
                      inc=sem, inc_k=16)

    def dout(dst, src, wait=(), sem="g"):
        return p.emit("gpsimd", lambda e: e.dma_start(out=dst, in_=src), wait=wait,
                      inc=sem, inc_k=16)

    t_id = din(ident[:], id_d, sem="c_id")
    t_o1 = p.emit("vector", lambda e: e.memset(ones1[:], 1.0), inc=V)
    t_oc = p.emit("vector", lambda e: e.memset(onescol[:], 1.0), inc=V)
    t_ocf = p.emit("vector", lambda e: e.memset(onescf[:], 1.0), inc=V)
    t_wz = p.emit("vector", lambda e: e.memset(wpb[:], 0.0), inc=V)

    t_vones = p.emit("gpsimd", lambda e: e.memset(
        bass.AP(v_tm, 64, [[32 * VSTR, 128], [VSTR, 32], [65, 16], [1, 1]]), 1.0),
        inc=G)

    def load_castw(dram, dstb, ncols, cols_off, tok_prev, tag):
        tok = tok_prev
        for ci in range(NCH):
            for h in range(ncols // 1024):
                tw = din(wldf[:], dram[ci * 128:(ci + 1) * 128, h * 1024:(h + 1) * 1024],
                         wait=(tok if isinstance(tok, list) else [tok]), sem=tag)
                tok = p.emit("vector", lambda e, ci=ci, h=h, ncols=ncols, dstb=dstb,
                             cols_off=cols_off: e.tensor_copy(
                    out=dstb[:, cols_off + ci * ncols + h * 1024:
                             cols_off + ci * ncols + (h + 1) * 1024],
                    in_=wldf[:]), wait=[tw], inc=V)
        return tok

    xring = Ring(xr)
    xnring = Ring(xnb)
    tr_war = [None]

    def ln_transpose_sweep(consume, extra_wait=None, src=None, ntg=8):
        if src is None:
            src = x_d
        for tg in range(ntg):
            ftoks = []
            for i in range(4):
                ti = tg * 4 + i
                buf, war = xring.next()
                tx = din(buf[:], src[ti * 128:(ti + 1) * 128, :],
                         wait=[war, extra_wait if ti == 0 else None],
                         sem=f"xl{xring.i}")
                tln = ln_emit(p, buf[:], C, stat, scr[:], wait=[tx])
                nb, nwar = xnring.next()
                txn = p.emit("vector", lambda e, buf=buf, nb=nb: e.tensor_scalar(
                    out=nb[:], in0=buf[:], scalar1=stat[:, 0:1], scalar2=stat[:, 1:2],
                    op0=ALU.subtract, op1=ALU.mult), wait=[tln, nwar], inc=V)
                xring.used(txn)
                for half in range(2):
                    tps = None
                    for j in range(4):
                        cc = half * 4 + j
                        tps = p.emit("tensor", lambda e, cc=cc, j=j, nb=nb: e.transpose(
                            tr_ps[:, j * 128:(j + 1) * 128],
                            nb[:, cc * 128:(cc + 1) * 128], ident[:]),
                            wait=[txn, t_id, tr_war[0]], inc=TT)
                    tev = p.emit("vector", lambda e, half=half, i=i: e.tensor_copy(
                        out=bass.AP(xnf, (half * 4) * 512 + i * 128,
                                    [[NCH * 512, 128], [512, 4], [1, 128]]),
                        in_=tr_ps[:]), wait=[tps], inc=V)
                    tr_war[0] = tev
                xnring.used(tr_war[0])
            consume(tg, p.tok(V))

    t_wqk = load_castw(wqk_d, arena, 2048, 0, None, "wl")
    t_wv = load_castw(wv_d, wpb, 1024, 0, [t_wqk, t_wz], "wl")

    def consume_a1(tg, t_f):
        for hp in range(NCH):
            pp = g_ps[hp % 2]
            war = p.tok(V)
            mm = None
            for cc in range(NCH):
                mm = p.emit("tensor", lambda e, cc=cc, hp=hp, pp=pp: e.matmul(
                    pp[:], wqkb[:, cc * 2048 + C + hp * 128:
                                cc * 2048 + C + (hp + 1) * 128],
                    xnf[:, cc * 512:(cc + 1) * 512],
                    start=(cc == 0), stop=(cc == NCH - 1)),
                    wait=[t_f, t_wqk, war], inc=(TT if cc == NCH - 1 else None))
            p.emit("vector", lambda e, pp=pp, hp=hp, tg=tg: e.tensor_copy(
                out=k_fm[:, hp * T + tg * 512: hp * T + tg * 512 + 512],
                in_=pp[:]), wait=[mm], inc=V)
        for i in range(4):
            kt = tg * 4 + i
            for half in range(2):
                pp = g_ps[half]
                war = p.tok(V)
                mm = None
                for cc in range(NCH):
                    mm = p.emit("tensor", lambda e, cc=cc, half=half, i=i, pp=pp: e.matmul(
                        pp[:], xnf[:, cc * 512 + i * 128: cc * 512 + (i + 1) * 128],
                        wpb[:, cc * C + half * 512: cc * C + (half + 1) * 512],
                        start=(cc == 0), stop=(cc == NCH - 1)),
                        wait=[t_f, t_wv, war], inc=(TT if cc == NCH - 1 else None))
                p.emit("vector", lambda e, kt=kt, half=half, pp=pp: e.tensor_copy(
                    out=bass.AP(v_tm, kt * VSTR + half * 520,
                                [[32 * VSTR, 128], [65, 8], [1, 64]]),
                    in_=pp[:]), wait=[mm], inc=V)

    qtok = [None]
    ln_transpose_sweep(consume_a1)

    sq_done = [None, None]
    qn_war = [None, None]

    def consume_q(tg, t_f):
        qevs = []
        for hp in range(NCH):
            pp = g_ps[hp % 2]
            war = p.tok(V)
            mm = None
            for cc in range(NCH):
                mm = p.emit("tensor", lambda e, cc=cc, hp=hp, pp=pp: e.matmul(
                    pp[:], wqkb[:, cc * 2048 + hp * 128:
                                cc * 2048 + (hp + 1) * 128],
                    xnf[:, cc * 512:(cc + 1) * 512],
                    start=(cc == 0), stop=(cc == NCH - 1)),
                    wait=[t_f, t_wqk, war, sq_done[hp % 2]],
                    inc=(TT if cc == NCH - 1 else None))
            tev = p.emit("vector", lambda e, pp=pp, hp=hp: e.tensor_copy(
                out=wpb[:, hp * 512:(hp + 1) * 512],
                in_=pp[:]),
                wait=[mm] + ([qtok[0], sq_done[0]] if hp == 0 else []), inc=V)
            qevs.append(tev)
            if _QN:
                qsq = wldf[:, 0:512]
                tsq = p.emit("scalar", lambda e, hp=hp, qsq=qsq: e.activation(
                    out=qsq, in_=wpb[:, hp * 512:(hp + 1) * 512], func=AF.Square),
                    wait=[tev, qn_war[0], qn_war[1]], inc=A)
                sq_done[0] = tsq
                for hh in range(2):
                    r0 = hh * 64
                    tnm = p.emit("tensor", lambda e, r0=r0, hh=hh, qsq=qsq: e.matmul(
                        av_ps[0][0:1, :], onescf[r0:r0 + 64, 0:1], qsq[r0:r0 + 64, :],
                        start=True, stop=True),
                        wait=[tsq, t_ocf, qn_war[0]], inc=TT)
                    qn_war[0] = p.emit("vector", lambda e, hh=hh, tg=tg,
                                       hp=hp: e.tensor_reduce(
                        out=qm2[0:1, tg * 16 + hp * 2 + hh: tg * 16 + hp * 2 + hh + 1],
                        in_=av_ps[0][0:1, :], axis=AX, op=ALU.max),
                        wait=[tnm], inc=V)
        qtok[0] = dout(
            bass.AP(q_scr, tg * 512 * NCH, [[NCH * OWN_T, 128], [1, NCH * 512]]),
            qtmp, wait=qevs, sem="qs")

    if not _QN:
        p.emit("vector", lambda e: e.memset(qm2[:], 1.0), inc=V)
    ln_transpose_sweep(consume_q, src=xq_d, ntg=4)
    t_qm2 = p.tok(V)

    t_phasea = p.tok(V)

    km_done = None
    kmax_war = [None, None]
    last_mm = t_phasea
    for qd in range(4):
        tsq = p.emit("scalar", lambda e, qd=qd: e.activation(
            out=arena[:, 0:8192], in_=k_fm[:, qd * 8192:(qd + 1) * 8192],
            func=AF.Square), wait=[t_phasea, last_mm], inc=A)
        for sub in range(2):
            hp_ = 2 * qd + sub
            for hh in range(2):
                h = hp_ * 2 + hh
                r0 = hh * 64
                red_toks = []
                for ktg in range(8):
                    pp = sc_ps[ktg % 2]
                    war = kmax_war[ktg % 2]
                    tmm = p.emit("tensor", lambda e, r0=r0, sub=sub, ktg=ktg, pp=pp: e.matmul(
                        pp[0:1, :], onescol[r0:r0 + 64, 0:1],
                        bass.AP(arena, r0 * 16384 + sub * 4096 + ktg * 512,
                                [[16384, 64], [1, 512]]),
                        start=True, stop=True),
                        wait=[tsq, t_oc, war], inc=TT)
                    last_mm = tmm
                    red_toks.append(p.emit("vector", lambda e, ktg=ktg, pp=pp: e.tensor_reduce(
                        out=mtmp[:, ktg:ktg + 1], in_=pp[0:1, :], axis=AX, op=ALU.max),
                        wait=[tmm, km_done if ktg == 0 else None], inc=V))
                    kmax_war[ktg % 2] = red_toks[-1]
                km_done = p.emit("vector", lambda e, h=h: e.tensor_reduce(
                    out=kmrow[:, h:h + 1], in_=mtmp[:], axis=AX, op=ALU.max),
                    wait=[red_toks[-1]], inc=V)
    t_lnkm = p.emit("scalar", lambda e: e.activation(
        out=lnkm[:], in_=kmrow[:], func=AF.Ln), wait=[km_done], inc=A)

    t_m = p.emit("scalar", lambda e: e.activation(
        out=mrow[:], in_=qm2[:], func=AF.Ln), wait=[t_qm2, t_lnkm], inc=A)
    t_l4 = None
    for g4 in range(4):
        t_l4 = p.emit("vector", lambda e, g4=g4: e.tensor_copy(
            out=lnkm4[0:1, g4 * 16:(g4 + 1) * 16], in_=lnkm[0:1, :]),
            wait=[t_lnkm, t_l4], inc=V)
    t_m = p.emit("vector", lambda e: e.tensor_tensor(
        out=mrow[:], in0=mrow[:], in1=lnkm4[:], op=ALU.add),
        wait=[t_m, t_l4], inc=V)
    t_m = p.emit("scalar", lambda e: e.activation(
        out=mrow[:], in_=mrow[:], func=AF.Exp, scale=np.float32(0.5),
        bias=np.float32(-2.0794415416798357)), wait=[t_m], inc=A)
    t_m = p.emit("vector", lambda e: e.tensor_scalar(
        out=mrow[:], in0=mrow[:], scalar1=-84.0, scalar2=0.0,
        op0=ALU.add, op1=ALU.max), wait=[t_m], inc=V)
    t_m = p.emit("vector", lambda e: e.tensor_scalar(
        out=mrow[:], in0=mrow[:], scalar1=-1.0, scalar2=None,
        op0=ALU.mult), wait=[t_m], inc=V)
    t_m = p.emit("tensor", lambda e: e.matmul(
        bc_ps[:, 0:64], ones1[0:1, :], mrow[0:1, :], start=True, stop=True),
        wait=[t_m, t_o1], inc=TT)
    t_mcol_all = p.emit("vector", lambda e: e.tensor_copy(
        out=mcol64[:], in_=bc_ps[:, 0:64]), wait=[t_m], inc=V)

    t_msk = din(maskb[:], mask_d, wait=[t_phasea], sem="c_mk")
    t_bp = din(bpb[:], bp_dh.ap(), wait=[t_phasea], sem="c_bp")
    t_wp = load_castw(wp_d, wpb, 1024, 0, [qtok[0], p.tok(A)], "wl")

    qring = Ring(qring_t)
    pring = Ring(probs_t)
    sc_war = [None, None, None, None]
    bc_war = [qtok[0]]
    oring = Ring(so)
    oring.war = [t_phasea, t_phasea]
    for bi in range(4):
        nkt = 8 * (bi + 1)
        qb, qwar = qring.next()
        t_q = din(qb, bass.AP(q_scr, bi * 512 * NCH, [[NCH * OWN_T, 128], [1, NCH * 512]]),
                  wait=[qwar, qtok[0], t_phasea if bi == 0 else None],
                  sem=f"ql{qring.i}")
        for hp in range(NCH):
            av_war = p.tok(V)
            av_done = [None, None]

            def emit_avs(ktx, ptoks):
                for hh in range(2):
                    pb, te, pidx = ptoks[hh]
                    hd = hp * 2 + hh
                    av = p.emit("tensor", lambda e, pb=pb, hd=hd, ktx=ktx,
                                hh=hh, nkt=nkt: e.matmul(
                        av_ps[hh][0:65, :],
                        bass.AP(v_tm, ktx * VSTR + hd * 65,
                                [[32 * VSTR, 128], [1, 65]]),
                        pb, start=(ktx == 0), stop=(ktx == nkt - 1)),
                        wait=[te, t_vones, (av_war if ktx == 0 else None)], inc=TT)
                    pring.used(av, pidx)
                    av_done[hh] = av

            pend = None
            for kt in range(nkt):
                bb = (kt % 2) * 2
                ptoks = []
                for hh in range(2):
                    pp = sc_ps[bb + hh]
                    r0 = hh * 64
                    mm = p.emit("tensor", lambda e, pp=pp, hp=hp, kt=kt, r0=r0, qb=qb: e.matmul(
                        pp[:], k_fm[r0:r0 + 64, hp * T + kt * 128: hp * T + (kt + 1) * 128],
                        bass.AP(qb.tensor, qb.offset + hp * 512 + r0 * 16384,
                                [[16384, 64], [1, 512]]),
                        start=True, stop=True, tile_position=(r0, 0)),
                        wait=[sc_war[bb + hh], t_q, t_mcol_all,
                              kmax_war[bb + hh] if bb + hh < 2 else None,
                              t_wp if kt == 0 and hp == 0 else None],
                        inc=TT)
                    pb, pwar, pidx = pring.next_i()
                    midx = bi * 16 + hp * 2 + hh
                    te = p.emit("scalar", lambda e, pp=pp, pb=pb, midx=midx: e.activation(
                        out=pb, in_=pp[:], func=AF.Exp, scale=np.float32(0.125),
                        bias=mcol64[:, midx:midx + 1]),
                        wait=[mm, pwar, t_mcol_all], inc=A)
                    sc_war[bb + hh] = te
                    if kt >= nkt - 8:
                        mp = kt - (nkt - 8)
                        te = p.emit("vector", lambda e, pb=pb, mp=mp: e.tensor_tensor(
                            out=pb, in0=pb, in1=maskb[:, mp * 512:(mp + 1) * 512],
                            op=ALU.mult), wait=[te, t_msk], inc=V)
                    ptoks.append((pb, te, pidx))
                if pend is not None:
                    emit_avs(kt - 1, pend)
                pend = ptoks
            emit_avs(nkt - 1, pend)
            td = p.emit("vector", lambda e: e.tensor_copy(
                out=dst2[:, 0:512], in_=av_ps[0][64:65, :]),
                wait=[av_done[0], t_wp], inc=V)
            td = p.emit("vector", lambda e: e.tensor_copy(
                out=dst2[:, 512:1024], in_=av_ps[1][64:65, :]), wait=[av_done[1], td],
                inc=V)
            td = p.emit("vector", lambda e: e.tensor_scalar(
                out=dst2[:], in0=dst2[:], scalar1=1e-37, scalar2=None,
                op0=ALU.add), wait=[td], inc=V)
            td = p.emit("vector", lambda e: e.reciprocal(
                out=dst2[:], in_=dst2[:]), wait=[td], inc=V)
            tb0 = p.emit("tensor", lambda e: e.matmul(
                bc_ps[0:64, :], ones1[0:1, 0:64], dst2[:, 0:512], start=True,
                stop=True, tile_position=(0, 0)), wait=[td, t_o1, bc_war[0]], inc=TT)
            tb1 = p.emit("tensor", lambda e: e.matmul(
                bc_ps[64:128, :], ones1[0:1, 0:64], dst2[:, 512:1024], start=True,
                stop=True, tile_position=(0, 64)), wait=[td], inc=TT)
            tu0 = p.emit("vector", lambda e, hp=hp: e.tensor_copy(
                out=a_un[0:64, :], in_=av_ps[0][0:64, :]), wait=[av_done[0], td,
                bc_war[0]], inc=V)
            tu1 = p.emit("vector", lambda e, hp=hp: e.tensor_copy(
                out=a_un[64:128, :], in_=av_ps[1][0:64, :]), wait=[av_done[1], td],
                inc=V)
            tn = p.emit("vector", lambda e, hp=hp: e.tensor_tensor(
                out=bass.AP(arena, 10240 + hp * 512, [[16384, 128], [1, 512]]),
                in0=a_un[:], in1=bc_ps[:], op=ALU.mult),
                wait=[tu0, tu1, tb0, tb1], inc=V)
            bc_war[0] = tn
        t_attn = p.tok(V)
        for i in range(4):
            ob, owar = oring.next()
            for half in range(2):
                pp = g_ps[half]
                mm = None
                for cc in range(NCH):
                    mm = p.emit("tensor", lambda e, cc=cc, half=half, i=i, pp=pp: e.matmul(
                        pp[:], bass.AP(arena, 10240 + cc * 512 + i * 128,
                                       [[16384, 128], [1, 128]]),
                        wpb[:, cc * C + half * 512: cc * C + (half + 1) * 512],
                        start=(cc == 0), stop=(cc == NCH - 1)),
                        wait=[t_attn, t_wp, sc_war[half]],
                        inc=(TT if cc == NCH - 1 else None))
                sc_war[half] = p.emit("vector", lambda e, half=half, pp=pp,
                                      ob=ob: e.tensor_tensor(
                    out=ob[:, half * 512:(half + 1) * 512], in0=pp[:],
                    in1=bpb[:, half * 512:(half + 1) * 512], op=ALU.add),
                    wait=[mm, t_bp, owar if half == 0 else None], inc=V)
            ts = dout(s_out_d[bi * 512 + i * 128: bi * 512 + (i + 1) * 128, :], ob[:],
                      wait=[p.tok(V)], sem=f"so{oring.i}")
            oring.used(ts)
        qring.used(p.tok("t"))


def _mixer_io(nc):
    return dict(
        x=nc.dram_tensor("x", [T, C], F32, kind="ExternalInput").ap(),
        xq=nc.dram_tensor("xq", [OWN_T, C], F32, kind="ExternalInput").ap(),
        wqk=nc.dram_tensor("wqk", [C, 2 * C], F32, kind="ExternalInput").ap(),
        wv=nc.dram_tensor("wv", [C, C], F32, kind="ExternalInput").ap(),
        wp=nc.dram_tensor("wp", [C, C], F32, kind="ExternalInput").ap(),
        bp_h=nc.dram_tensor("bp", [128, C], F32, kind="ExternalInput"),
        masks=nc.dram_tensor("masks", [128, 8 * 512], BF16,
                             kind="ExternalInput").ap(),
        ident=nc.dram_tensor("ident", [128, 128], BF16, kind="ExternalInput").ap())


def build_l0_mixer():
    nc = bass.Bass("TRN2", target_bir_lowering=False)
    es = ExitStack()
    p = Prog(nc, es)
    io = _mixer_io(nc)
    io["s_out"] = nc.dram_tensor("s_out", [OWN_T, C], F32,
                                 kind="ExternalOutput").ap()
    io["qscr_h"] = nc.dram_tensor("qscr", [128, NCH * OWN_T], BF16,
                                  kind="Internal")
    emit_l0_mixer(nc, p, es, io)
    p.emit("sync", None, wait=[p.tok("so0"), p.tok("so1")] +
           ([p.tok("dbg")] if DBGM else []))
    p.build()
    es.close()
    return nc



def emit_l0_ffn(nc, p, sbes, io, tag="f0"):
    sb = lambda name, shape, dt=F32: sbes.enter_context(
        nc.sbuf_tensor(tag + name, shape, dt))
    pst = lambda name, shape, dt=F32: sbes.enter_context(
        nc.psum_tensor(tag + name, shape, dt))
    NT = OWN_T // 128

    x_d = io["xq"]
    s_d = io["s_out"]
    w1_d = io["w1T"]
    w2_d = io["w2T"]
    b1_dh = io["b1_h"]
    b2_dh = io["b2_h"]
    gam_dh = io["gamma_h"]
    bet_dh = io["beta_h"]
    id_d = io["ident"]
    incr_d = io["incr"]
    h1_scr = io["h1scr_h"]

    wq = sb("wq", [128, NCH * HID], BF16)
    bigscr = sb("bigscr", [128, 12288], F32)
    wldf = [bigscr[:, 0:4096], bigscr[:, 4096:8192]]
    wscr = bigscr[:, 8192:12288]
    h1b = [bigscr[:, 0:4096], bigscr[:, 4096:8192]]
    yo = [sb(f"incrb{i}", [128, C], BF16) for i in range(2)]
    absacc = sb("absacc", [128, 40], F32)
    onescol = sb("onescol", [128, 1], F32)
    ones1 = sb("ones1", [1, 128], F32)
    cells = sb("cells", [1, 8], F32)
    swc = sb("swc", [128, 1], F32)
    kc_ = sb("kc_", [128, 1], F32)
    b1b = sb("b1b", [128, HID], F32)
    b2b = sb("b2b", [128, C], F32)
    gamb = sb("gamb", [128, C], F32)
    betb = sb("betb", [128, C], F32)
    ident = sb("identsb", [128, 128], BF16)
    xr = [sb(f"xr{i}", [128, C], F32) for i in range(2)]
    sr = [sb(f"sr{i}", [128, C], F32) for i in range(2)]
    xn = sb("xn", [128, C], F32)
    h0 = sb("h0", [128, C], F32)
    stat = sb("stat", [128, 8], F32)
    scr = sb("scr", [128, C], F32)
    xq1 = sb("xq1", [128, C], BF16)
    xq1f = sb("xq1f", [128, C], BF16)
    gsc = sb("gsc", [128, 512], F32)
    rm2 = sb("rm2", [128, 8], F32)
    rm2all = sb("rm2all", [128, NT], F32)
    xq2s = sb("xq2s", [128, 512], BF16)
    xq2f = sb("xq2f", [128, HID], BF16)
    g2 = sb("g2", [128, C], F32)

    mm_ps = [pst(f"mmps{i}", [128, 512], F32) for i in range(2)]
    tr_ps = [pst(f"trps{i}", [128, 512], BF16) for i in range(2)]
    red_ps = pst("redps", [1, 1], F32)
    bc_ps = pst("bcps", [128, 1], F32)

    LD, V, A, TT, G = "ld", "v", "a", "t", "g"

    def din(dst, src, wait=(), sem="ld"):
        return p.emit("sync", lambda e: e.dma_start(out=dst, in_=src), wait=wait,
                      inc=sem, inc_k=16)

    def dout(dst, src, wait=(), sem="g"):
        return p.emit("gpsimd", lambda e: e.dma_start(out=dst, in_=src), wait=wait,
                      inc=sem, inc_k=16)

    t_id = din(ident[:], id_d, sem="c_id")
    t_b1 = din(b1b[:], b1_dh.ap(), sem="c_b1")
    t_b2 = din(b2b[:], b2_dh.ap(), sem="c_b2")
    t_gam = din(gamb[:], gam_dh.ap(), sem="c_gm")
    t_bet = din(betb[:], bet_dh.ap(), sem="c_bt")
    t_ones = p.emit("vector", lambda e: e.memset(onescol[:], 1.0), inc=V)
    t_o1 = p.emit("vector", lambda e: e.memset(ones1[:], 1.0), inc=V)

    wring = Ring(wldf)

    def emit_absmean(dram, nchunks, fdim, col0, pre=()):
        tok = None
        pre = list(pre) if isinstance(pre, (list, tuple)) else [pre]
        for ci in range(nchunks):
            buf, war = wring.next()
            tw = din(buf[:, 0:fdim], dram[ci * 128:(ci + 1) * 128, :],
                     wait=[war] + (pre if ci == 0 else []), sem=f"wl{wring.i}")
            tok = p.emit("scalar", lambda e, buf=buf, ci=ci, fdim=fdim: e.activation(
                out=wscr[:, 0:fdim], in_=buf[:, 0:fdim], func=AF.Abs,
                accum_out=absacc[:, col0 + ci:col0 + ci + 1]), wait=[tw, tok], inc=A)
            wring.used(tok)
        return tok

    def emit_sw(ncols, col0, cell, tok_in):
        tv = p.emit("vector", lambda e, ncols=ncols, col0=col0: e.tensor_reduce(
            out=stat[:, 0:1], in_=absacc[:, col0:col0 + ncols], axis=AX, op=ALU.add),
            wait=[tok_in, t_ones], inc=V)
        tt = p.emit("tensor", lambda e: e.matmul(
            red_ps[:], stat[:, 0:1], onescol[:], start=True, stop=True),
            wait=[tv], inc=TT)
        tv2 = p.emit("vector", lambda e, cell=cell: e.tensor_scalar(
            out=cells[:, cell:cell + 1], in0=red_ps[:], scalar1=1.0 / (C * HID),
            scalar2=1e-5, op0=ALU.mult, op1=ALU.max), wait=[tt], inc=V)
        return p.emit("vector", lambda e, cell=cell: e.reciprocal(
            out=cells[:, cell:cell + 1], in_=cells[:, cell:cell + 1]),
            wait=[tv2], inc=V)

    def bcast_scalar(cell_col, dst_col, wait):
        tm_ = p.emit("tensor", lambda e, cell_col=cell_col: e.matmul(
            bc_ps[:], ones1[0:1, :], cell_col, start=True, stop=True),
            wait=list(wait) + [t_o1], inc=TT)
        return p.emit("vector", lambda e, dst_col=dst_col: e.tensor_copy(
            out=dst_col, in_=bc_ps[:]), wait=[tm_], inc=V)

    def emit_wquant(dram, nchunks, fdim, tok_sw):
        tok = None
        for ci in range(nchunks):
            buf, war = wring.next()
            tw = din(buf[:, 0:fdim], dram[ci * 128:(ci + 1) * 128, :], wait=[war],
                     sem=f"wl{wring.i}")
            t1 = p.emit("vector", lambda e, buf=buf, fdim=fdim: e.tensor_scalar(
                out=wscr[:, 0:fdim], in0=buf[:, 0:fdim], scalar1=swc[:, 0:1],
                scalar2=1.49, op0=ALU.mult, op1=ALU.min), wait=[tw, tok_sw, tok],
                inc=V)
            wring.used(t1)
            t2 = p.emit("vector", lambda e, fdim=fdim: e.tensor_scalar(
                out=wscr[:, 0:fdim], in0=wscr[:, 0:fdim], scalar1=-1.49,
                scalar2=MAGIC, op0=ALU.max, op1=ALU.add), wait=[t1], inc=V)
            tok = p.emit("vector", lambda e, ci=ci, fdim=fdim: e.tensor_scalar(
                out=wq[:, ci * fdim:(ci + 1) * fdim], in0=wscr[:, 0:fdim],
                scalar1=-MAGIC, scalar2=None, op0=ALU.add), wait=[t2], inc=V)
        return tok

    def emit_kscale(cell, tok):
        t1 = bcast_scalar(cells[:, cell:cell + 1], swc[:], [tok])
        t2 = p.emit("vector", lambda e: e.tensor_scalar(
            out=kc_[:], in0=swc[:], scalar1=127.0, scalar2=None, op0=ALU.mult),
            wait=[t1], inc=V)
        return t1, p.emit("vector", lambda e: e.reciprocal(out=kc_[:], in_=kc_[:]),
                          wait=[t2], inc=V)

    ta1 = emit_absmean(w1_d, NCH, HID, 0)
    tsw1 = emit_sw(NCH, 0, 0, ta1)
    t_swb1, tk1 = emit_kscale(0, tsw1)
    twq1 = emit_wquant(w1_d, NCH, HID, t_swb1)

    xring = Ring(xr)
    h1ring = Ring(h1b)
    mmps_free = [None, None]
    rm_toks = []
    h1_store_toks = []
    g_guard = [None]
    for i in range(NT):
        buf, war = xring.next()
        t_x = din(buf[:], x_d[i * 128:(i + 1) * 128, :], wait=[war],
                  sem=f"xl{xring.i}")
        tln = ln_emit(p, buf[:], C, stat, scr[:], wait=[t_x])
        t_xn = p.emit("vector", lambda e, buf=buf: e.tensor_scalar(
            out=xn[:], in0=buf[:], scalar1=stat[:, 0:1], scalar2=stat[:, 1:2],
            op0=ALU.subtract, op1=ALU.mult), wait=[tln], inc=V)
        xring.used(t_xn)
        tln2 = ln_emit(p, xn[:], C, stat, scr[:], wait=[t_xn])
        t_h0 = p.emit("vector", lambda e: e.tensor_scalar(
            out=h0[:], in0=xn[:], scalar1=stat[:, 0:1], scalar2=stat[:, 1:2],
            op0=ALU.subtract, op1=ALU.mult), wait=[tln2], inc=V)
        t_h0 = p.emit("vector", lambda e: e.tensor_tensor(
            out=h0[:], in0=h0[:], in1=gamb[:], op=ALU.mult), wait=[t_h0, t_gam], inc=V)
        t_h0 = p.emit("vector", lambda e: e.tensor_tensor(
            out=h0[:], in0=h0[:], in1=betb[:], op=ALU.add), wait=[t_h0, t_bet], inc=V)
        t_m = p.emit("vector", lambda e: e.tensor_reduce(
            out=stat[:, 4:5], in_=h0[:], axis=AX, op=ALU.max,
            apply_absolute_value=True), wait=[t_h0], inc=V)
        t_m = p.emit("vector", lambda e: e.tensor_scalar(
            out=stat[:, 4:5], in0=stat[:, 4:5], scalar1=1e-5, scalar2=None,
            op0=ALU.max), wait=[t_m], inc=V)
        t_rs = p.emit("vector", lambda e: e.tensor_tensor(
            out=stat[:, 5:6], in0=stat[:, 4:5], in1=kc_[:], op=ALU.mult),
            wait=[t_m, tk1], inc=V)
        t_sx = p.emit("vector", lambda e: e.reciprocal(
            out=stat[:, 6:7], in_=stat[:, 4:5]), wait=[t_m], inc=V)
        t_sx = p.emit("vector", lambda e: e.tensor_scalar(
            out=stat[:, 6:7], in0=stat[:, 6:7], scalar1=127.0, scalar2=None,
            op0=ALU.mult), wait=[t_sx], inc=V)
        t_q = p.emit("vector", lambda e: e.tensor_scalar(
            out=scr[:], in0=h0[:], scalar1=stat[:, 6:7], scalar2=MAGIC,
            op0=ALU.mult, op1=ALU.add), wait=[t_sx], inc=V)
        t_q = p.emit("vector", lambda e: e.tensor_scalar(
            out=xq1[:], in0=scr[:], scalar1=-MAGIC, scalar2=None, op0=ALU.add),
            wait=[t_q], inc=V)
        trtok = None
        for half in range(2):
            tps = None
            for j in range(4):
                cc = half * 4 + j
                tps = p.emit("tensor", lambda e, cc=cc, j=j, half=half: e.transpose(
                    tr_ps[half][:, j * 128:(j + 1) * 128],
                    xq1[:, cc * 128:(cc + 1) * 128], ident[:]),
                    wait=[t_q, t_id, trtok], inc=TT)
            trtok = p.emit("vector", lambda e, half=half: e.tensor_copy(
                out=xq1f[:, half * 512:(half + 1) * 512], in_=tr_ps[half][:]),
                wait=[tps], inc=V)
        t_xq1f = trtok
        h1, h1war = h1ring.next()
        if i < 2:
            h1war = twq1
        rm_tok = None
        for ns in range(HID // 512):
            pp = mm_ps[ns % 2]
            mm = None
            for cc in range(NCH):
                mm = p.emit("tensor", lambda e, cc=cc, ns=ns, pp=pp: e.matmul(
                    pp[:], xq1f[:, cc * 128:(cc + 1) * 128],
                    wq[:, cc * HID + ns * 512: cc * HID + (ns + 1) * 512],
                    start=(cc == 0), stop=(cc == NCH - 1)),
                    wait=[t_xq1f, twq1, mmps_free[ns % 2]],
                    inc=(TT if cc == NCH - 1 else None))
            t_ev = p.emit("vector", lambda e, pp=pp, ns=ns: e.scalar_tensor_tensor(
                out=gsc[:], in0=pp[:], scalar=stat[:, 5:6],
                in1=b1b[:, ns * 512:(ns + 1) * 512], op0=ALU.mult, op1=ALU.add),
                wait=[mm, t_rs, t_b1, g_guard[0]], inc=V)
            mmps_free[ns % 2] = t_ev
            t_g = p.emit("scalar", lambda e, ns=ns, h1=h1: e.activation(
                out=h1[:, ns * 512:(ns + 1) * 512], in_=gsc[:], func=AF.Gelu),
                wait=[t_ev, h1war if ns == 0 else None], inc=A)
            g_guard[0] = t_g
            rm_tok = p.emit("vector", lambda e, ns=ns, h1=h1: e.tensor_reduce(
                out=rm2[:, ns:ns + 1], in_=h1[:, ns * 512:(ns + 1) * 512],
                axis=AX, op=ALU.max, apply_absolute_value=True), wait=[t_g], inc=V)
        t_rm = p.emit("vector", lambda e, i=i: e.tensor_reduce(
            out=rm2all[:, i:i + 1], in_=rm2[:], axis=AX, op=ALU.max),
            wait=[rm_tok], inc=V)
        rm_toks.append(t_rm)
        th1 = dout(h1_scr.ap()[i * 128:(i + 1) * 128, :], h1[:], wait=[t_rm],
                   sem=f"h1s{h1ring.i}")
        h1_store_toks.append(th1)
        h1ring.used(th1)

    t_passa = [p.tok("h1s0"), p.tok("h1s1")]

    ta2 = emit_absmean(w2_d, 32, C, 8, pre=t_passa)
    tsw2 = emit_sw(32, 8, 1, ta2)
    t_swb2, tk2 = emit_kscale(1, tsw2)
    twq2 = emit_wquant(w2_d, 32, C, t_swb2)

    sring = Ring(sr)
    yring = Ring(yo)
    sl_guard = [g_guard[0]]
    for i in range(NT):
        h1, h1war = h1ring.next()
        if i < 2:
            h1war = twq2
        t_h1 = din(h1[:], h1_scr.ap()[i * 128:(i + 1) * 128, :],
                   wait=[h1war, h1_store_toks[i]], sem=f"h1l{h1ring.i}")
        t_m = p.emit("vector", lambda e, i=i: e.tensor_scalar(
            out=stat[:, 4:5], in0=rm2all[:, i:i + 1], scalar1=1e-5, scalar2=None,
            op0=ALU.max), wait=[rm_toks[i]], inc=V)
        t_rs = p.emit("vector", lambda e: e.tensor_tensor(
            out=stat[:, 5:6], in0=stat[:, 4:5], in1=kc_[:], op=ALU.mult),
            wait=[t_m, tk2], inc=V)
        t_sx = p.emit("vector", lambda e: e.reciprocal(
            out=stat[:, 6:7], in_=stat[:, 4:5]), wait=[t_m], inc=V)
        t_sx = p.emit("vector", lambda e: e.tensor_scalar(
            out=stat[:, 6:7], in0=stat[:, 6:7], scalar1=127.0, scalar2=None,
            op0=ALU.mult), wait=[t_sx], inc=V)
        trtok = None
        for sl in range(8):
            t_q = p.emit("vector", lambda e, sl=sl, h1=h1: e.tensor_scalar(
                out=gsc[:], in0=h1[:, sl * 512:(sl + 1) * 512],
                scalar1=stat[:, 6:7], scalar2=MAGIC, op0=ALU.mult, op1=ALU.add),
                wait=[t_sx, t_h1, sl_guard[0]], inc=V)
            t_q = p.emit("vector", lambda e: e.tensor_scalar(
                out=xq2s[:], in0=gsc[:], scalar1=-MAGIC, scalar2=None, op0=ALU.add),
                wait=[t_q], inc=V)
            tps = None
            for j in range(4):
                cc = sl * 4 + j
                tps = p.emit("tensor", lambda e, j=j, sl=sl: e.transpose(
                    tr_ps[sl % 2][:, j * 128:(j + 1) * 128],
                    xq2s[:, j * 128:(j + 1) * 128], ident[:]),
                    wait=[t_q, t_id, trtok], inc=TT)
            trtok = p.emit("vector", lambda e, sl=sl: e.tensor_copy(
                out=xq2f[:, sl * 512:(sl + 1) * 512], in_=tr_ps[sl % 2][:]),
                wait=[tps], inc=V)
        t_xq2f = trtok
        h1ring.used(t_xq2f)
        sbuf, swar = sring.next()
        t_s = din(sbuf[:], s_d[i * 128:(i + 1) * 128, :],
                  wait=[swar if i >= 2 else p.tok(V)], sem=f"sl{sring.i}")
        ybuf, ywar = yring.next()
        for ns in range(2):
            pp = mm_ps[ns]
            mm = None
            for cc in range(32):
                mm = p.emit("tensor", lambda e, cc=cc, ns=ns, pp=pp: e.matmul(
                    pp[:], xq2f[:, cc * 128:(cc + 1) * 128],
                    wq[:, cc * C + ns * 512: cc * C + (ns + 1) * 512],
                    start=(cc == 0), stop=(cc == 31)),
                    wait=[t_xq2f, twq2, mmps_free[ns]],
                    inc=(TT if cc == 31 else None))
            t_ev = p.emit("vector", lambda e, pp=pp, ns=ns: e.scalar_tensor_tensor(
                out=gsc[:], in0=pp[:], scalar=stat[:, 5:6],
                in1=b2b[:, ns * 512:(ns + 1) * 512], op0=ALU.mult, op1=ALU.add),
                wait=[mm, t_rs, t_b2, sl_guard[0]], inc=V)
            mmps_free[ns] = t_ev
            t_sl = p.emit("scalar", lambda e, ns=ns: e.activation(
                out=g2[:, ns * 512:(ns + 1) * 512], in_=gsc[:], func=AF.Silu),
                wait=[t_ev], inc=A)
            sl_guard[0] = t_sl
            t_y = p.emit("vector", lambda e, ns=ns, sbuf=sbuf, ybuf=ybuf: e.tensor_tensor(
                out=ybuf[:, ns * 512:(ns + 1) * 512],
                in0=sbuf[:, ns * 512:(ns + 1) * 512],
                in1=g2[:, ns * 512:(ns + 1) * 512], op=ALU.mult),
                wait=[t_sl, t_s, ywar if ns == 0 else None], inc=V)
        tlast = p.tok(V)
        sring.used(tlast)
        ty = dout(incr_d[i * 128:(i + 1) * 128, :], ybuf[:], wait=[tlast],
                  sem=f"xo{yring.i}")
        yring.used(ty)


def build_l0_ffn():
    nc = bass.Bass("TRN2", target_bir_lowering=False)
    es = ExitStack()
    p = Prog(nc, es)
    io = dict(
        xq=nc.dram_tensor("x", [OWN_T, C], F32, kind="ExternalInput").ap(),
        s_out=nc.dram_tensor("s", [OWN_T, C], F32, kind="ExternalInput").ap(),
        w1T=nc.dram_tensor("w1T", [C, HID], F32, kind="ExternalInput").ap(),
        w2T=nc.dram_tensor("w2T", [HID, C], F32, kind="ExternalInput").ap(),
        b1_h=nc.dram_tensor("b1", [128, HID], F32, kind="ExternalInput"),
        b2_h=nc.dram_tensor("b2", [128, C], F32, kind="ExternalInput"),
        gamma_h=nc.dram_tensor("gamma", [128, C], F32, kind="ExternalInput"),
        beta_h=nc.dram_tensor("beta", [128, C], F32, kind="ExternalInput"),
        ident=nc.dram_tensor("ident", [128, 128], BF16, kind="ExternalInput").ap(),
        incr=nc.dram_tensor("incr", [OWN_T, C], BF16, kind="ExternalOutput").ap(),
        h1scr_h=nc.dram_tensor("h1scr", [OWN_T, HID], F32, kind="Internal"))
    emit_l0_ffn(nc, p, es, io)
    p.emit("sync", None, wait=[p.tok("xo0"), p.tok("xo1")])
    p.build()
    es.close()
    return nc



def emit_l1_mixer(nc, p, sbes, io, tag="r1"):
    sb = lambda name, shape, dt=F32: sbes.enter_context(
        nc.sbuf_tensor(tag + name, shape, dt))
    pst = lambda name, shape, dt=F32: sbes.enter_context(
        nc.psum_tensor(tag + name, shape, dt))

    HT = T // 4
    STR = HT + 1

    x_d = io["x"]
    incr_d = io["incrfull"]
    td_dh = io["td_h"]
    tf_dh = io["tf_h"]
    tm_dh = io["tm_h"]
    wk_d = io["wkT"]
    wv_d = io["wvT"]
    wr_d = io["wrT"]
    wo_d = io["woT"]
    id_d = io["ident"]
    s1_d = io["s1"]

    wkb = sb("wkb", [128, NCH * C], BF16)
    wvb = sb("wvb", [128, NCH * C], BF16)
    wrb = sb("wrb", [128, NCH * C], BF16)
    wob = sb("wob", [128, NCH * C], BF16)
    wldf = [sb(f"wldf{i}", [128, C], F32) for i in range(2)]
    pcol = sb("pcol", [128, 8], F32)
    dcol = sb("dcol", [128, 8], F32)
    fwcol = sb("fwcol", [128, 8], F32)
    tmcol = sb("tmcol", [128, 8], F32)
    tm1col = sb("tm1col", [128, 8], F32)
    kvstate = sb("kvstate", [128, 8], F32)
    kstate = sb("kstate", [128, 8], F32)
    bndcol = sb("bndcol", [128, 8], F32)
    xbuf = sb("xbuf", [128, 9 * STR], F32)
    xmb = sb("xmb", [128, NCH * HT], BF16)
    rwkvb = sb("rwkvb", [128, NCH * HT], BF16)
    xr = [sb(f"xr{i}", [128, C], F32) for i in range(2)]
    ib = [sb(f"ib{i}", [128, C], BF16) for i in range(2)]
    xnb = [sb(f"xnb{i}", [128, C], F32) for i in range(2)]
    stat = sb("stat", [128, 8], F32)
    scr = sb("scr", [128, C], F32)
    scr2 = sb("scr2", [128, HT], F32)
    kbuf = [sb(f"kbuf{i}", [128, HT], F32) for i in range(2)]
    _kv1 = sb("kvbuf", [128, HT], F32)
    kvbuf = [_kv1, _kv1]
    _wv1 = sb("wkvb", [128, HT + 1], F32)
    wkvb = [_wv1, _wv1]
    _wk1 = sb("wkbuf", [128, HT + 1], F32)
    wkbuf = [_wk1, _wk1]
    _dt1 = sb("dtile", [128, HT], F32)
    dtile = [_dt1, _dt1]
    sigb = [sb(f"sigb{i}", [128, 512], F32) for i in range(2)]
    ident = sb("identsb", [128, 128], BF16)
    identf = sb("identf", [128, 128], F32)
    so = [sb(f"so{i}", [128, C], F32) for i in range(2)]

    g_ps = [pst(f"gps{i}", [128, 512], F32) for i in range(2)]
    tr_ps = [pst(f"trps{i}", [128, 512], F32) for i in range(2)]

    LD, V, A, TT, G = "ld", "v", "a", "t", "g"

    def din(dst, src, wait=(), sem="ld"):
        return p.emit("sync", lambda e: e.dma_start(out=dst, in_=src), wait=wait,
                      inc=sem, inc_k=16)

    def dout(dst, src, wait=(), sem="g"):
        return p.emit("gpsimd", lambda e: e.dma_start(out=dst, in_=src), wait=wait,
                      inc=sem, inc_k=16)

    t_id = din(ident[:], id_d, sem="c_id")
    t_idf = p.emit("vector", lambda e: e.tensor_copy(out=identf[:], in_=ident[:]),
                   wait=[t_id], inc=V)

    wring = Ring(wldf)

    def load_castw(dram, dstb):
        tok = None
        for ci in range(NCH):
            buf, war = wring.next()
            tw = din(buf[:], dram[ci * 128:(ci + 1) * 128, :], wait=[war],
                     sem=f"wl{wring.i}")
            tok = p.emit("vector", lambda e, buf=buf, ci=ci, dstb=dstb: e.tensor_copy(
                out=dstb[:, ci * C:(ci + 1) * C], in_=buf[:]), wait=[tw], inc=V)
            wring.used(tok)
        return tok

    t_wk = load_castw(wk_d, wkb)
    t_wv = load_castw(wv_d, wvb)
    t_wr = load_castw(wr_d, wrb)
    t_wo = load_castw(wo_d, wob)

    t_td = din(pcol[:], td_dh.ap(), sem="c_pa")
    t_d = p.emit("scalar", lambda e: e.activation(
        out=dcol[:], in_=pcol[:], func=AF.Exp), wait=[t_td], inc=A)
    t_d = p.emit("scalar", lambda e: e.activation(
        out=dcol[:], in_=dcol[:], func=AF.Exp, scale=np.float32(-1.0)),
        wait=[t_d], inc=A)
    t_tf = din(pcol[:], tf_dh.ap(), wait=[t_d], sem="c_pa")
    t_fw = p.emit("scalar", lambda e: e.activation(
        out=fwcol[:], in_=pcol[:], func=AF.Exp), wait=[t_tf], inc=A)
    t_tm = din(tmcol[:], tm_dh.ap(), sem="c_tm")
    t_tm1 = p.emit("vector", lambda e: e.tensor_scalar(
        out=tm1col[:], in0=tmcol[:], scalar1=-1.0, scalar2=1.0, op0=ALU.mult,
        op1=ALU.add), wait=[t_tm], inc=V)
    t_st0 = p.emit("vector", lambda e: e.memset(kvstate[:], 0.0), inc=V)
    t_st1 = p.emit("vector", lambda e: e.memset(kstate[:], 0.0), inc=V)
    st_toks = [[t_st0] * NCH, [t_st1] * NCH]

    xring = Ring(xr)
    iring = Ring(ib)
    xnring = Ring(xnb)
    oring = Ring(so)
    tr_war = [None, None]
    bnd_tok = [None]
    ps_free = [None, None]
    kb_free = [None, None]
    kv_free = [None, None]
    wv_free = [None, None]
    wk_free = [None, None]
    dt_free = [None, None]
    sg_free = [None, None]

    for hf in range(4):
        if hf == 0:
            t_bnd = p.emit("vector", lambda e: e.memset(
                bass.AP(xbuf, STR, [[9 * STR, 128], [STR, 8], [1, 1]]), 0.0), inc=V)
        else:
            t_bnd = p.emit("vector", lambda e: e.tensor_copy(
                out=bass.AP(xbuf, STR, [[9 * STR, 128], [STR, 8], [1, 1]]),
                in_=bndcol[:]), wait=[bnd_tok[0], p.tok(V)], inc=V)

        for i in range(HT // 128):
            ti = hf * (HT // 128) + i
            buf, war = xring.next()
            tx = din(buf[:], x_d[ti * 128:(ti + 1) * 128, :], wait=[war],
                     sem=f"xl{xring.i}")
            ibuf, iwar = iring.next()
            txi = din(ibuf[:], incr_d[ti * 128:(ti + 1) * 128, :], wait=[iwar],
                      sem=f"il{iring.i}")
            tcv = p.emit("vector", lambda e, ibuf=ibuf: e.tensor_copy(
                out=scr[:], in_=ibuf[:]), wait=[txi, p.tok(V)], inc=V)
            tadd = p.emit("vector", lambda e, buf=buf: e.tensor_tensor(
                out=buf[:], in0=buf[:], in1=scr[:], op=ALU.add),
                wait=[tx, tcv], inc=V)
            iring.used(tadd)
            tln = ln_emit(p, buf[:], C, stat, scr[:], wait=[tadd])
            nb, nwar = xnring.next()
            txn = p.emit("vector", lambda e, buf=buf, nb=nb: e.tensor_scalar(
                out=nb[:], in0=buf[:], scalar1=stat[:, 0:1], scalar2=stat[:, 1:2],
                op0=ALU.subtract, op1=ALU.mult), wait=[tln, nwar], inc=V)
            xring.used(txn)
            for half in range(2):
                trp = tr_ps[half]
                tps = None
                for j in range(4):
                    cc = half * 4 + j
                    tps = p.emit("tensor", lambda e, cc=cc, j=j, nb=nb, trp=trp: e.transpose(
                        trp[:, j * 128:(j + 1) * 128],
                        nb[:, cc * 128:(cc + 1) * 128], identf[:]),
                        wait=[txn, t_idf, tr_war[half]], inc=TT)
                tev = p.emit("vector", lambda e, half=half, i=i, trp=trp: e.tensor_copy(
                    out=bass.AP(xbuf, (half * 4 + 1) * STR + 1 + i * 128,
                                [[9 * STR, 128], [STR, 4], [1, 128]]),
                    in_=trp[:]), wait=[tps, t_bnd], inc=V)
                tr_war[half] = tev
            xnring.used(tr_war[1])
        t_x1f = p.tok(V)
        if hf < 3:
            bnd_tok[0] = p.emit("vector", lambda e: e.tensor_copy(
                out=bndcol[:],
                in_=bass.AP(xbuf, STR + HT, [[9 * STR, 128], [STR, 8], [1, 1]])),
                wait=[t_x1f], inc=V)

        t_xm = None
        for cc in range(NCH):
            t_a = p.emit("vector", lambda e, cc=cc: e.tensor_scalar(
                out=scr2[:], in0=xbuf[:, (cc + 1) * STR + 1:(cc + 1) * STR + 1 + HT],
                scalar1=tmcol[:, cc:cc + 1], scalar2=None, op0=ALU.mult),
                wait=[t_x1f, t_tm, t_xm], inc=V)
            t_xm = p.emit("vector", lambda e, cc=cc: e.scalar_tensor_tensor(
                out=xmb[:, cc * HT:(cc + 1) * HT],
                in0=xbuf[:, (cc + 1) * STR:(cc + 1) * STR + HT],
                scalar=tm1col[:, cc:cc + 1], in1=scr2[:], op0=ALU.mult, op1=ALU.add),
                wait=[t_a, t_tm1], inc=V)
        t_xmf = p.tok(V)

        for cc in range(NCH):
            par = (hf * NCH + cc) % 2
            kbu, kvu = kbuf[par], kvbuf[par]
            wkvu, wku, dtu = wkvb[par], wkbuf[par], dtile[par]
            t_dt = p.emit("vector", lambda e, dtu=dtu: e.memset(dtu[:], 1.0),
                          wait=[dt_free[par]], inc=V)
            t_dt = p.emit("vector", lambda e, cc=cc, dtu=dtu: e.tensor_scalar(
                out=dtu[:], in0=dtu[:], scalar1=dcol[:, cc:cc + 1], scalar2=None,
                op0=ALU.mult), wait=[t_dt, t_d], inc=V)
            ev_toks = []
            for ts4 in range(HT // 512):
                pp = g_ps[ts4 % 2]
                mm = None
                for kc in range(NCH):
                    mm = p.emit("tensor", lambda e, kc=kc, cc=cc, ts4=ts4, pp=pp: e.matmul(
                        pp[:], wkb[:, kc * C + cc * 128: kc * C + (cc + 1) * 128],
                        xmb[:, kc * HT + ts4 * 512: kc * HT + (ts4 + 1) * 512],
                        start=(kc == 0), stop=(kc == NCH - 1)),
                        wait=[t_xmf, t_wk, ps_free[ts4 % 2]],
                        inc=(TT if kc == NCH - 1 else None))
                ev = p.emit("vector", lambda e, ts4=ts4, pp=pp, kbu=kbu: e.tensor_scalar(
                    out=kbu[:, ts4 * 512:(ts4 + 1) * 512], in0=pp[:], scalar1=60.0,
                    scalar2=None, op0=ALU.min), wait=[mm, kb_free[par]], inc=V)
                ps_free[ts4 % 2] = ev
                ev_toks.append(ev)
            t_k = p.emit("scalar", lambda e, kbu=kbu: e.activation(
                out=kbu[:], in_=kbu[:], func=AF.Exp), wait=ev_toks, inc=A)
            for ts4 in range(HT // 512):
                pp = g_ps[ts4 % 2]
                mm = None
                for kc in range(NCH):
                    mm = p.emit("tensor", lambda e, kc=kc, cc=cc, ts4=ts4, pp=pp: e.matmul(
                        pp[:], wvb[:, kc * C + cc * 128: kc * C + (cc + 1) * 128],
                        xmb[:, kc * HT + ts4 * 512: kc * HT + (ts4 + 1) * 512],
                        start=(kc == 0), stop=(kc == NCH - 1)),
                        wait=[t_xmf, t_wv, ps_free[ts4 % 2]],
                        inc=(TT if kc == NCH - 1 else None))
                t_kv = p.emit("vector", lambda e, ts4=ts4, pp=pp, kbu=kbu,
                              kvu=kvu: e.tensor_tensor(
                    out=kvu[:, ts4 * 512:(ts4 + 1) * 512],
                    in0=kbu[:, ts4 * 512:(ts4 + 1) * 512], in1=pp[:], op=ALU.mult),
                    wait=[mm, t_k, kv_free[par]], inc=V)
                ps_free[ts4 % 2] = t_kv
            t_c0 = p.emit("vector", lambda e, cc=cc, wkvu=wkvu: e.tensor_copy(
                out=wkvu[:, 0:1], in_=kvstate[:, cc:cc + 1]),
                wait=[t_kv, st_toks[0][cc], wv_free[par]], inc=V)
            t_sc = p.emit("vector", lambda e, wkvu=wkvu, dtu=dtu,
                          kvu=kvu: e.tensor_tensor_scan(
                out=wkvu[:, 1:HT + 1], data0=dtu[:], data1=kvu[:],
                initial=wkvu[:, 0:1], op0=ALU.mult, op1=ALU.add),
                wait=[t_c0, t_dt], inc=V)
            st_toks[0][cc] = p.emit("vector", lambda e, cc=cc, wkvu=wkvu: e.tensor_copy(
                out=kvstate[:, cc:cc + 1], in_=wkvu[:, HT:HT + 1]), wait=[t_sc], inc=V)
            t_c1 = p.emit("vector", lambda e, cc=cc, wku=wku: e.tensor_copy(
                out=wku[:, 0:1], in_=kstate[:, cc:cc + 1]),
                wait=[st_toks[1][cc], wk_free[par]], inc=V)
            t_sk = p.emit("vector", lambda e, wku=wku, dtu=dtu,
                          kbu=kbu: e.tensor_tensor_scan(
                out=wku[:, 1:HT + 1], data0=dtu[:], data1=kbu[:],
                initial=wku[:, 0:1], op0=ALU.mult, op1=ALU.add),
                wait=[t_c1, t_dt, t_k], inc=V)
            st_toks[1][cc] = p.emit("vector", lambda e, cc=cc, wku=wku: e.tensor_copy(
                out=kstate[:, cc:cc + 1], in_=wku[:, HT:HT + 1]), wait=[t_sk], inc=V)
            dt_free[par] = t_sk
            t_wkv = p.emit("vector", lambda e, cc=cc, kvu=kvu,
                           wkvu=wkvu: e.scalar_tensor_tensor(
                out=kvu[:], in0=kvu[:], scalar=fwcol[:, cc:cc + 1],
                in1=wkvu[:, 0:HT], op0=ALU.mult, op1=ALU.add),
                wait=[t_sc, t_fw], inc=V)
            t_wkf = p.emit("vector", lambda e, cc=cc, kbu=kbu,
                           wku=wku: e.scalar_tensor_tensor(
                out=kbu[:], in0=kbu[:], scalar=fwcol[:, cc:cc + 1],
                in1=wku[:, 0:HT], op0=ALU.mult, op1=ALU.add),
                wait=[t_sk, t_fw], inc=V)
            wv_free[par] = t_wkv
            wk_free[par] = t_wkf
            t_r = p.emit("scalar", lambda e, kbu=kbu: e.activation(
                out=kbu[:], in_=kbu[:], func=AF.Ln, bias=np.float32(1e-16)),
                wait=[t_wkf], inc=A)
            t_r = p.emit("scalar", lambda e, kbu=kbu: e.activation(
                out=kbu[:], in_=kbu[:], func=AF.Exp, scale=np.float32(-1.0)),
                wait=[t_r], inc=A)
            t_rat = p.emit("vector", lambda e, kvu=kvu, kbu=kbu: e.tensor_tensor(
                out=kvu[:], in0=kvu[:], in1=kbu[:], op=ALU.mult),
                wait=[t_r, t_wkv], inc=V)
            kb_free[par] = t_rat
            last_rw = None
            for ts4 in range(HT // 512):
                pp = g_ps[ts4 % 2]
                sg = sigb[ts4 % 2]
                mm = None
                for kc in range(NCH):
                    mm = p.emit("tensor", lambda e, kc=kc, cc=cc, ts4=ts4, pp=pp: e.matmul(
                        pp[:], wrb[:, kc * C + cc * 128: kc * C + (cc + 1) * 128],
                        xmb[:, kc * HT + ts4 * 512: kc * HT + (ts4 + 1) * 512],
                        start=(kc == 0), stop=(kc == NCH - 1)),
                        wait=[t_xmf, t_wr, ps_free[ts4 % 2]],
                        inc=(TT if kc == NCH - 1 else None))
                t_sg = p.emit("scalar", lambda e, pp=pp, sg=sg: e.activation(
                    out=sg[:], in_=pp[:], func=AF.Sigmoid),
                    wait=[mm, sg_free[ts4 % 2]], inc=A)
                ps_free[ts4 % 2] = t_sg
                last_rw = p.emit("vector", lambda e, cc=cc, ts4=ts4, sg=sg,
                                 kvu=kvu: e.tensor_tensor(
                    out=rwkvb[:, cc * HT + ts4 * 512: cc * HT + (ts4 + 1) * 512],
                    in0=sg[:], in1=kvu[:, ts4 * 512:(ts4 + 1) * 512],
                    op=ALU.mult), wait=[t_sg, t_rat], inc=V)
                sg_free[ts4 % 2] = last_rw
            kv_free[par] = last_rw
        t_rw = p.tok(V)

        for j in range(HT // 128):
            ob, owar = oring.next()
            evs = []
            for ns in range(2):
                pp = g_ps[ns]
                mm = None
                for cc in range(NCH):
                    mm = p.emit("tensor", lambda e, cc=cc, ns=ns, j=j, pp=pp: e.matmul(
                        pp[:], rwkvb[:, cc * HT + j * 128: cc * HT + (j + 1) * 128],
                        wob[:, cc * C + ns * 512: cc * C + (ns + 1) * 512],
                        start=(cc == 0), stop=(cc == NCH - 1)),
                        wait=[t_rw, t_wo, ps_free[ns]],
                        inc=(TT if cc == NCH - 1 else None))
                ps_free[ns] = p.emit("scalar", lambda e, ns=ns, pp=pp, ob=ob: e.copy(
                    out=ob[:, ns * 512:(ns + 1) * 512], in_=pp[:]),
                    wait=[mm, owar if ns == 0 else None], inc=A)
                evs.append(ps_free[ns])
            ts_ = dout(s1_d[(hf * (HT // 128) + j) * 128:(hf * (HT // 128) + j + 1) * 128, :],
                       ob[:], wait=evs, sem=f"so{oring.i}")
            oring.used(ts_)


def build_l1_mixer():
    nc = bass.Bass("TRN2", target_bir_lowering=False)
    es = ExitStack()
    p = Prog(nc, es)
    io = dict(
        x=nc.dram_tensor("x", [T, C], F32, kind="ExternalInput").ap(),
        incrfull=nc.dram_tensor("incrfull", [T, C], BF16,
                                kind="ExternalInput").ap(),
        td_h=nc.dram_tensor("td", [128, 8], F32, kind="ExternalInput"),
        tf_h=nc.dram_tensor("tf", [128, 8], F32, kind="ExternalInput"),
        tm_h=nc.dram_tensor("tm", [128, 8], F32, kind="ExternalInput"),
        wkT=nc.dram_tensor("wkT", [C, C], F32, kind="ExternalInput").ap(),
        wvT=nc.dram_tensor("wvT", [C, C], F32, kind="ExternalInput").ap(),
        wrT=nc.dram_tensor("wrT", [C, C], F32, kind="ExternalInput").ap(),
        woT=nc.dram_tensor("woT", [C, C], F32, kind="ExternalInput").ap(),
        ident=nc.dram_tensor("ident", [128, 128], BF16, kind="ExternalInput").ap(),
        s1=nc.dram_tensor("s1", [T, C], F32, kind="ExternalOutput").ap())
    emit_l1_mixer(nc, p, es, io)
    p.emit("sync", None, wait=[p.tok("so0"), p.tok("so1")])
    p.build()
    es.close()
    return nc



def emit_l1_ffn(nc, p, sbes, io, tag="f1"):
    sb = lambda name, shape, dt=F32: sbes.enter_context(
        nc.sbuf_tensor(tag + name, shape, dt))
    pst = lambda name, shape, dt=F32: sbes.enter_context(
        nc.psum_tensor(tag + name, shape, dt))
    NT = T // 128

    x_d = io["x"]
    incr_d = io["incrfull"]
    s1_d = io["s1"]
    kern_d = io["kern"]
    isc_dh = io["iscale_h"]
    ksc_dh = io["kscale_h"]
    bias_dh = io["bias_h"]
    id_d = io["ident"]
    out_d = io["out"]

    kb = sb("kb", [128, NCH * C], BF16)
    kq = sb("kq", [128, NCH * C], F8)
    cells = sb("cells", [1, 4], F32)
    rsin_c = sb("rsin_c", [128, 1], F32)
    scmb_c = sb("scmb_c", [128, 1], F32)
    rsk_c = sb("rsk_c", [128, 1], F32)
    biasb = sb("biasb", [128, C], BF16)
    ident = sb("identsb", [128, 128], BF16)
    xr = [sb(f"xr{i}", [128, C], F32) for i in range(2)]
    ib = [sb(f"ib{i}", [128, C], BF16) for i in range(2)]
    sr = [sb(f"sr{i}", [128, C], F32) for i in range(2)]
    xnb = sb("xnb", [128, C], BF16)
    xnf = sb("xnf", [128, C], BF16)
    xqf = sb("xqf", [128, C], F8)
    yb = sb("yb", [128, C], BF16)
    sgb = sb("sgb", [128, C], BF16)
    gf = sb("gf", [128, C], F32)
    stat = sb("stat", [128, 8], F32)
    scr = sb("scr", [128, C], F32)
    yo = [sb(f"yo{i}", [128, C], F32) for i in range(2)]

    mm_ps = [pst(f"mmps{i}", [128, 512], F32) for i in range(2)]
    tr_ps = [pst(f"trps{i}", [128, 512], BF16) for i in range(2)]
    bc_ps = pst("bcps", [128, 1], F32)
    ones1 = sb("ones1", [1, 128], F32)

    LD, V, A, TT, G = "ld", "v", "a", "t", "g"

    def din(dst, src, wait=(), sem="ld"):
        return p.emit("sync", lambda e: e.dma_start(out=dst, in_=src), wait=wait,
                      inc=sem, inc_k=16)

    def dout(dst, src, wait=(), sem="g"):
        return p.emit("gpsimd", lambda e: e.dma_start(out=dst, in_=src), wait=wait,
                      inc=sem, inc_k=16)

    t_id = din(ident[:], id_d, sem="c_id")
    t_bb = din(biasb[:], bias_dh.ap(), sem="c_bb")
    t_is = din(cells[:, 0:1], isc_dh.ap()[0:1, 0:1], sem="c_is")
    t_ks = din(cells[:, 1:2], ksc_dh.ap()[0:1, 0:1], sem="c_ks")
    t_o1 = p.emit("vector", lambda e: e.memset(ones1[:], 1.0), inc=V)
    t_c = p.emit("vector", lambda e: e.tensor_scalar(
        out=cells[:, 2:3], in0=cells[:, 0:1], scalar1=cells[0:1, 1:2], scalar2=None,
        op0=ALU.mult), wait=[t_is, t_ks], inc=V)
    t_c = p.emit("vector", lambda e: e.reciprocal(
        out=cells[:, 0:1], in_=cells[:, 0:1]), wait=[t_c], inc=V)
    t_c = p.emit("vector", lambda e: e.reciprocal(
        out=cells[:, 1:2], in_=cells[:, 1:2]), wait=[t_c], inc=V)

    def bcast_scalar(cell_col, dst_col, wait):
        tm_ = p.emit("tensor", lambda e: e.matmul(
            bc_ps[:], ones1[0:1, :], cell_col, start=True, stop=True),
            wait=list(wait) + [t_o1], inc=TT)
        return p.emit("vector", lambda e: e.tensor_copy(out=dst_col, in_=bc_ps[:]),
                      wait=[tm_], inc=V)

    t_rsin = bcast_scalar(cells[:, 0:1], rsin_c[:], [t_c])
    t_rsk = bcast_scalar(cells[:, 1:2], rsk_c[:], [t_rsin, t_c])
    t_scmb = bcast_scalar(cells[:, 2:3], scmb_c[:], [t_rsk, t_c])

    tkq = None
    for ci in range(NCH):
        tw = din(kb[:, ci * C:(ci + 1) * C], kern_d[ci * 128:(ci + 1) * 128, :],
                 sem=f"kb{ci}")
        tkq = p.emit("vector", lambda e, ci=ci: e.tensor_scalar(
            out=kq[:, ci * C:(ci + 1) * C], in0=kb[:, ci * C:(ci + 1) * C],
            scalar1=rsk_c[:, 0:1], scalar2=None, op0=ALU.mult),
            wait=[tw, t_rsk], inc=V)

    xring = Ring(xr)
    iring = Ring(ib)
    sring = Ring(sr)
    yring = Ring(yo)
    mmps_free = [None, None]
    for i in range(NT):
        xbuf, xwar = xring.next()
        t_x = din(xbuf[:], x_d[i * 128:(i + 1) * 128, :], wait=[xwar],
                  sem=f"xl{xring.i}")
        ibuf, iwar = iring.next()
        txi = din(ibuf[:], incr_d[i * 128:(i + 1) * 128, :], wait=[iwar],
                  sem=f"il{iring.i}")
        tcv = p.emit("vector", lambda e, ibuf=ibuf: e.tensor_copy(
            out=scr[:], in_=ibuf[:]), wait=[txi, p.tok(V)], inc=V)
        t_x = p.emit("vector", lambda e, xbuf=xbuf: e.tensor_tensor(
            out=xbuf[:], in0=xbuf[:], in1=scr[:], op=ALU.add),
            wait=[t_x, tcv], inc=V)
        iring.used(t_x)
        tln = ln_emit(p, xbuf[:], C, stat, scr[:], wait=[t_x])
        t_xn = p.emit("vector", lambda e, xbuf=xbuf: e.tensor_scalar(
            out=xnb[:], in0=xbuf[:], scalar1=stat[:, 0:1], scalar2=stat[:, 1:2],
            op0=ALU.subtract, op1=ALU.mult), wait=[tln], inc=V)
        trtok = None
        for half in range(2):
            tps = None
            for j in range(4):
                cc = half * 4 + j
                tps = p.emit("tensor", lambda e, cc=cc, j=j, half=half: e.transpose(
                    tr_ps[half][:, j * 128:(j + 1) * 128],
                    xnb[:, cc * 128:(cc + 1) * 128], ident[:]),
                    wait=[t_xn, t_id, trtok], inc=TT)
            trtok = p.emit("vector", lambda e, half=half: e.tensor_copy(
                out=xnf[:, half * 512:(half + 1) * 512], in_=tr_ps[half][:]),
                wait=[tps], inc=V)
        t_q = p.emit("vector", lambda e: e.tensor_scalar(
            out=xqf[:], in0=xnf[:], scalar1=rsin_c[:, 0:1], scalar2=None,
            op0=ALU.mult), wait=[trtok, t_rsin], inc=V)
        sbuf, swar = sring.next()
        t_s = din(sbuf[:], s1_d[i * 128:(i + 1) * 128, :], wait=[swar],
                  sem=f"sl{sring.i}")
        ybuf, ywar = yring.next()
        for ns in range(2):
            pp = mm_ps[ns]
            mm = None
            for cc in range(NCH):
                mm = p.emit("tensor", lambda e, cc=cc, ns=ns, pp=pp: e.matmul(
                    pp[:], xqf[:, cc * 128:(cc + 1) * 128],
                    kq[:, cc * C + ns * 512: cc * C + (ns + 1) * 512],
                    start=(cc == 0), stop=(cc == NCH - 1)),
                    wait=[t_q, tkq, mmps_free[ns]],
                    inc=(TT if cc == NCH - 1 else None))
            mmps_free[ns] = p.emit("vector", lambda e, ns=ns, pp=pp: e.tensor_scalar(
                out=yb[:, ns * 512:(ns + 1) * 512], in0=pp[:], scalar1=scmb_c[:, 0:1],
                scalar2=None, op0=ALU.mult), wait=[mm, t_scmb], inc=V)
        t_y = p.emit("vector", lambda e: e.tensor_tensor(
            out=yb[:], in0=yb[:], in1=biasb[:], op=ALU.add), wait=[p.tok(V), t_bb],
            inc=V)
        t_sg = p.emit("scalar", lambda e: e.activation(
            out=sgb[:], in_=yb[:], func=AF.Sigmoid), wait=[t_y], inc=A)
        t_g = p.emit("vector", lambda e: e.tensor_tensor(
            out=yb[:], in0=yb[:], in1=sgb[:], op=ALU.mult), wait=[t_sg], inc=V)
        t_g = p.emit("vector", lambda e: e.tensor_copy(out=gf[:], in_=yb[:]),
                     wait=[t_g], inc=V)
        t_o = p.emit("vector", lambda e, sbuf=sbuf, ybuf=ybuf: e.tensor_tensor(
            out=ybuf[:], in0=sbuf[:], in1=gf[:], op=ALU.mult),
            wait=[t_g, t_s, ywar], inc=V)
        t_o = p.emit("vector", lambda e, xbuf=xbuf, ybuf=ybuf: e.tensor_tensor(
            out=ybuf[:], in0=ybuf[:], in1=xbuf[:], op=ALU.add), wait=[t_o], inc=V)
        xring.used(t_o)
        sring.used(t_o)
        ty = dout(out_d[i * 128:(i + 1) * 128, :], ybuf[:], wait=[t_o],
                  sem=f"xo{yring.i}")
        yring.used(ty)


def build_l1_ffn():
    nc = bass.Bass("TRN2", target_bir_lowering=False)
    es = ExitStack()
    p = Prog(nc, es)
    io = dict(
        x=nc.dram_tensor("x", [T, C], F32, kind="ExternalInput").ap(),
        incrfull=nc.dram_tensor("incrfull", [T, C], BF16,
                                kind="ExternalInput").ap(),
        s1=nc.dram_tensor("s1", [T, C], F32, kind="ExternalInput").ap(),
        kern=nc.dram_tensor("kern", [C, C], BF16, kind="ExternalInput").ap(),
        iscale_h=nc.dram_tensor("iscale", [1, 1], F32, kind="ExternalInput"),
        kscale_h=nc.dram_tensor("kscale", [1, 1], F32, kind="ExternalInput"),
        bias_h=nc.dram_tensor("bias", [128, C], BF16, kind="ExternalInput"),
        ident=nc.dram_tensor("ident", [128, 128], BF16, kind="ExternalInput").ap(),
        out=nc.dram_tensor("out", [T, C], F32, kind="ExternalOutput").ap())
    emit_l1_ffn(nc, p, es, io)
    p.emit("sync", None, wait=[p.tok("xo0"), p.tok("xo1")])
    p.build()
    es.close()
    return nc



def _barrier(p):
    toks = [p.tok(nm) for nm in list(p.sems)]
    for e in ENGINES:
        p.emit(e, None, wait=toks)


def build_fused():
    nc = bass.Bass("TRN2", target_bir_lowering=False)
    es = ExitStack()
    p = Prog(nc, es)
    io = _mixer_io(nc)
    io["s_out"] = nc.dram_tensor("s0scr", [OWN_T, C], F32, kind="Internal").ap()
    io["qscr_h"] = nc.dram_tensor("qscr", [128, NCH * OWN_T], BF16,
                                  kind="Internal")
    io["w1T"] = nc.dram_tensor("w1T", [C, HID], F32, kind="ExternalInput").ap()
    io["w2T"] = nc.dram_tensor("w2T", [HID, C], F32, kind="ExternalInput").ap()
    io["b1_h"] = nc.dram_tensor("b1", [128, HID], F32, kind="ExternalInput")
    io["b2_h"] = nc.dram_tensor("b2", [128, C], F32, kind="ExternalInput")
    io["gamma_h"] = nc.dram_tensor("gamma", [128, C], F32, kind="ExternalInput")
    io["beta_h"] = nc.dram_tensor("beta", [128, C], F32, kind="ExternalInput")
    incr_t = nc.dram_tensor("incr", [OWN_T, C], BF16, kind="Internal")
    io["incr"] = incr_t.ap()
    io["h1scr_h"] = nc.dram_tensor("h1scr", [OWN_T, HID], F32, kind="Internal")
    incrfull_t = nc.dram_tensor("incrfull", [T, C], BF16, kind="Internal")
    io["incrfull"] = incrfull_t.ap()
    io["td_h"] = nc.dram_tensor("td", [128, 8], F32, kind="ExternalInput")
    io["tf_h"] = nc.dram_tensor("tf", [128, 8], F32, kind="ExternalInput")
    io["tm_h"] = nc.dram_tensor("tm", [128, 8], F32, kind="ExternalInput")
    io["wkT"] = nc.dram_tensor("wkT", [C, C], F32, kind="ExternalInput").ap()
    io["wvT"] = nc.dram_tensor("wvT", [C, C], F32, kind="ExternalInput").ap()
    io["wrT"] = nc.dram_tensor("wrT", [C, C], F32, kind="ExternalInput").ap()
    io["woT"] = nc.dram_tensor("woT", [C, C], F32, kind="ExternalInput").ap()
    io["s1"] = nc.dram_tensor("s1scr", [T, C], F32, kind="Internal").ap()
    io["kern"] = nc.dram_tensor("kern", [C, C], BF16, kind="ExternalInput").ap()
    io["iscale_h"] = nc.dram_tensor("iscale", [1, 1], F32, kind="ExternalInput")
    io["kscale_h"] = nc.dram_tensor("kscale", [1, 1], F32, kind="ExternalInput")
    io["bias_h"] = nc.dram_tensor("bias", [128, C], BF16, kind="ExternalInput")
    io["out"] = nc.dram_tensor("out", [T, C], F32, kind="ExternalOutput").ap()

    st = ExitStack()
    emit_l0_mixer(nc, p, st, io, tag="m1")
    st.close()
    _barrier(p)
    st = ExitStack()
    emit_l0_ffn(nc, p, st, io, tag="f0")
    st.close()
    _barrier(p)
    for i in range(4):
        p.emit("gpsimd", lambda e, i=i: e.collective_compute(
            "AllGather", ALU.bypass,
            replica_groups=[[0, 1], [2, 3], [4, 5], [6, 7]],
            ins=[incr_t.ap()[i * 512:(i + 1) * 512, :].opt()],
            outs=[incrfull_t.ap()[i * 1024:(i + 1) * 1024, :].opt()]),
            inc="cc")
    _barrier(p)
    st = ExitStack()
    emit_l1_mixer(nc, p, st, io, tag="r1")
    st.close()
    _barrier(p)
    st = ExitStack()
    emit_l1_ffn(nc, p, st, io, tag="f1")
    st.close()
    p.emit("sync", None, wait=[p.tok("xo0"), p.tok("xo1")])
    p.build()
    es.close()
    return nc



_NC_CACHE = {}


def _get_nc(name, builder, *args):
    key = (name,) + args
    if key not in _NC_CACHE:
        _NC_CACHE[key] = builder(*args)
    return _NC_CACHE[key]


def _causal_masks(role):
    q = np.arange(512)[None, :]
    k = np.arange(128)[:, None]
    diag = [(q - p * 128 - k >= 0).astype(bfdt) for p in range(4)]
    ones = np.ones((128, 512), bfdt)
    zero = np.zeros((128, 512), bfdt)
    tiles = diag + [zero] * 4 if role == 0 else [ones] * 4 + diag
    return np.concatenate(tiles, axis=1)


def kernel(**inputs):
    f32 = np.float32
    x = np.ascontiguousarray(np.asarray(inputs["L0_attn_x"], f32))
    w_qkv = np.asarray(inputs["L0_attn_w_qkv"], f32)
    wq = np.ascontiguousarray(w_qkv[:, 0:64, :].reshape(C, C).T)
    wk = np.ascontiguousarray(w_qkv[:, 64:128, :].reshape(C, C).T)
    wv = np.ascontiguousarray(w_qkv[:, 128:192, :].reshape(C, C).T)
    wqk = np.ascontiguousarray(np.concatenate([wq, wk], axis=1))
    wp = np.ascontiguousarray(np.asarray(inputs["L0_attn_w_proj"], f32).T)

    def brd(v):
        return np.ascontiguousarray(np.broadcast_to(np.asarray(v, f32)[None, :], (128, v.shape[-1])))

    bp = brd(inputs["L0_attn_b_proj"])
    w1T = np.ascontiguousarray(np.asarray(inputs["L0_ffn_w1"], f32).T)
    w2T = np.ascontiguousarray(np.asarray(inputs["L0_ffn_w2"], f32).T)
    b1 = brd(inputs["L0_ffn_b1"])
    b2 = brd(inputs["L0_ffn_b2"])
    gamma = brd(inputs["L0_ffn_gamma"])
    beta = brd(inputs["L0_ffn_beta"])

    def pcols(v):
        return np.ascontiguousarray(np.asarray(v, f32).reshape(8, 128).T)

    td = pcols(inputs["L1_attn_time_decay"])
    tf = pcols(inputs["L1_attn_time_first"])
    tm = pcols(inputs["L1_attn_time_mix"])
    wkT = np.ascontiguousarray(np.asarray(inputs["L1_attn_Wk"], f32).T)
    wvT = np.ascontiguousarray(np.asarray(inputs["L1_attn_Wv"], f32).T)
    wrT = np.ascontiguousarray(np.asarray(inputs["L1_attn_Wr"], f32).T)
    woT = np.ascontiguousarray(np.asarray(inputs["L1_attn_Wo"], f32).T)
    kern = np.ascontiguousarray(np.asarray(inputs["L1_ffn_kernel"]).astype(bfdt))
    iscale = np.asarray(inputs["L1_ffn_input_scale"], f32).reshape(1, 1)
    kscale = np.asarray(inputs["L1_ffn_kernel_scale"], f32).reshape(1, 1)
    bias = np.ascontiguousarray(np.broadcast_to(
        np.asarray(inputs["L1_ffn_bias"]).astype(bfdt)[None, :], (128, C)))

    ident = np.eye(128, dtype=bfdt)
    masks = [_causal_masks(0), _causal_masks(1)]

    def own_rows(arr_batch, core):
        return np.ascontiguousarray(np.concatenate(
            [arr_batch[blk * BLK:(blk + 1) * BLK] for blk in OWN_BLOCKS[core % 2]],
            axis=0))

    ncf = _get_nc("fused", build_fused)
    maps = [{"x": x[c // 2], "xq": own_rows(x[c // 2], c), "wqk": wqk, "wv": wv,
             "wp": wp, "bp": bp, "masks": masks[c % 2], "ident": ident,
             "w1T": w1T, "w2T": w2T, "b1": b1, "b2": b2, "gamma": gamma,
             "beta": beta, "td": td, "tf": tf, "tm": tm, "wkT": wkT,
             "wvT": wvT, "wrT": wrT, "woT": woT, "kern": kern,
             "iscale": iscale, "kscale": kscale, "bias": bias}
            for c in range(8)]
    res = run_bass_kernel_spmd(ncf, maps, core_ids=list(range(8)))
    out = np.stack([np.asarray(res.results[2 * b]["out"], f32) for b in range(B)])
    return out
